# version 06 (final)
# baseline (speedup 1.0000x reference)
import math
import numpy as np
from contextlib import ExitStack

import concourse.bass as bass
import concourse.mybir as mybir
from concourse.bass_utils import run_bass_kernel_spmd

F32 = mybir.dt.float32
BF16 = mybir.dt.bfloat16
I32 = mybir.dt.int32
AX = mybir.AxisListType.X
ALU = mybir.AluOpType
ACT = mybir.ActivationFunctionType

NT = 4096
NTILE = 8
TW = 512
EPS_LN = 1e-5
GROUPS = [[0, 1, 2, 3], [4, 5, 6, 7]]


class Prog:
    def __init__(self):
        self.ops = []
        self.d = 0
        self.c = 0

    def dma(self, out, in_, eng='sync', slow=False, batch=False):
        d0, c0 = self.d, self.c
        self.ops.append((eng, 'dma' if not batch else 'dmab',
                         (lambda e, out=out, in_=in_, slow=slow: e.dma_start(
                             out=out, in_=in_, allow_slow_non_contiguous=slow)), d0, c0))
        self.d += 1

    def comp(self, eng, fn):
        d0, c0 = self.d, self.c
        self.ops.append((eng, 'comp', fn, d0, c0))
        self.c += 1

    def coll(self, kind, op, ins, outs):
        d0, c0 = self.d, self.c
        self.ops.append(('gpsimd', 'comp',
                         (lambda e, k=kind, o=op, i=ins, u=outs:
                          e.collective_compute(k, o, ins=i, outs=u, replica_groups=GROUPS)), d0, c0))
        self.c += 1

    def emit(self, block, gd, gc):
        ops2 = []
        run_start = None
        for op in self.ops:
            eng, kind, fn, d0, c0 = op
            if kind in ('dma', 'dmab'):
                if kind == 'dmab' and run_start is not None and run_start[0] == eng:
                    ops2.append((eng, 'dma', fn, run_start[1], run_start[2]))
                else:
                    run_start = (eng, d0, c0)
                    ops2.append((eng, 'dma', fn, d0, c0))
            else:
                run_start = None
                ops2.append(op)
        per = {k: [] for k in ('sync', 'tensor', 'vector', 'scalar', 'gpsimd')}
        for op in ops2:
            per[op[0]].append(op)
        total_d, total_c = self.d, self.c

        def make(engname):
            oplist = per[engname]

            def body(eng):
                last_d = -1
                last_c = -1
                for (_, kind, fn, d0, c0) in oplist:
                    if d0 > 0 and d0 * 16 > last_d:
                        eng.wait_ge(gd, d0 * 16)
                        last_d = d0 * 16
                    if c0 > 0 and c0 > last_c:
                        eng.wait_ge(gc, c0)
                        last_c = c0
                    inst = fn(eng)
                    if kind == 'dma':
                        inst.then_inc(gd, 16)
                    else:
                        inst.then_inc(gc, 1)
                if engname == 'sync':
                    eng.wait_ge(gd, total_d * 16)
                    eng.wait_ge(gc, total_c)
            return body

        block.sync(make('sync'))
        block.tensor(make('tensor'))
        block.vector(make('vector'))
        block.scalar(make('scalar'))
        block.gpsimd(make('gpsimd'))


def build(nc):
    P = Prog()

    def di(name, shape, dtype=F32, kind="ExternalInput"):
        return nc.dram_tensor(name, shape, dtype, kind=kind).ap()

    x0 = di("x", [8, 128, NT])
    xg = di("xg", [8, 128, 256])
    wq = di("wq", [1024, 1024]); wk = di("wk", [1024, 1024])
    wv = di("wv", [1024, 1024]); wo = di("wo", [1024, 1024])
    rt_dm = di("rt_dm", [128, 8])
    gl_dm = di("gl_dm", [128, 8]); gh_dm = di("gh_dm", [128, 8])
    w1lT = di("w1lT", [1024, 512]); b1l_dm = di("b1l_dm", [128, 4])
    w2lT = di("w2lT", [512, 1024]); b2l_dm = di("b2l_dm", [128, 8])
    w1hT = di("w1hT", [1024, 4096]); b1h_dm = di("b1h_dm", [128, 32])
    w2hT = di("w2hT", [4096, 1024]); b2h_dm = di("b2h_dm", [128, 8])
    cpe1w = di("cpe1w", [128, 24]); cpe1b = di("cpe1b", [128, 8])
    n1g = di("n1g", [128, 8]); n1b = di("n1b", [128, 8])
    inw = di("inw", [1024, 1024]); inb = di("inb", [128, 8])
    actw = di("actw", [1024, 1024]); actb = di("actb", [128, 8])
    dwcw = di("dwcw", [128, 24]); dwcb = di("dwcb", [128, 8])
    qkw = di("qkw", [1024, 2048]); qkb = di("qkb", [128, 16])
    lepw = di("lepw", [128, 24]); lepb = di("lepb", [128, 8])
    outw = di("outw", [1024, 1024]); outb = di("outb", [128, 8])
    cpe2w = di("cpe2w", [128, 24]); cpe2b = di("cpe2b", [128, 8])
    n2g = di("n2g", [128, 8]); n2b = di("n2b", [128, 8])
    fc1w = di("fc1w", [1024, 4096]); fc1b = di("fc1b", [128, 32])
    fc2w = di("fc2w", [4096, 1024]); fc2b = di("fc2b", [128, 8])
    wT = di("wT", [1024, 1024])
    ident_bf = di("ident_bf", [128, 128], BF16)
    ident_f = di("ident_f", [128, 128])
    maskband = di("maskband", [128, 1920])
    ltri = di("ltri", [128, 128])
    rowsel = di("rowsel", [128, 32])
    hsel_l = di("hsel_l", [10, 1]); hsel_r = di("hsel_r", [10, 1])
    iota512 = di("iota512", [1, 512])

    out = di("out", [8, 128, NT], F32, kind="ExternalOutput")

    def scr(name, nchunk, dtype):
        return di(name, [nchunk, 128, NT], dtype, kind="Internal")
    LN0 = scr("LN0", 8, BF16)
    Qb = scr("Qb", 8, BF16); Kb = scr("Kb", 8, BF16); Vb = scr("Vb", 8, BF16)
    AOG = scr("AOG", 8, F32)
    AOB = scr("AOB", 8, BF16)
    X1 = scr("X1", 8, F32)
    LIN = scr("LIN", 8, BF16); HIN = scr("HIN", 8, BF16)
    HL = scr("HL", 4, BF16); HH = scr("HH", 32, BF16)
    X2 = scr("X2", 8, F32)
    XN = scr("XN", 8, F32)
    XC = scr("XC", 8, F32)
    XN1 = scr("XN1", 8, BF16)
    ACTR = scr("ACTR", 8, BF16)
    XI = scr("XI", 8, F32)
    XIS = scr("XIS", 8, F32); XISb = scr("XISb", 8, BF16)
    QKE = scr("QKE", 16, F32)
    QRS = scr("QRS", 8, BF16)
    AMUL = scr("AMUL", 8, BF16)
    XM = scr("XM", 8, F32)
    XM2 = scr("XM2", 8, F32)
    FCIN = scr("FCIN", 8, BF16)
    H1 = scr("H1", 32, BF16)
    X3 = scr("X3", 8, F32)
    XN4 = scr("XN4", 8, F32)
    PART = scr("PART", 8, F32)
    LOUT = scr("LOUT", 8, F32)
    s_in = di("s_in", [1, NT], F32, kind="Internal")
    s_out = di("s_out", [1, 16384], F32, kind="Internal")
    selbuf = di("selbuf", [32, 128], F32, kind="Internal")
    arow_h = di("arow_h", [1, NT], F32, kind="Internal")
    selrow = di("selrow", [1, NT], F32, kind="Internal")
    h_in = [di(f"h_in{i}", [2, 1024], F32, kind="Internal") for i in range(4)]
    h_out = [di(f"h_out{i}", [8, 1024], F32, kind="Internal") for i in range(4)]
    km_in = di("km_in", [128, 8], F32, kind="Internal")
    km_out = di("km_out", [128, 8], F32, kind="Internal")
    kv_in = di("kv_in", [64, 1024], F32, kind="Internal")
    kv_out = di("kv_out", [64, 1024], F32, kind="Internal")

    es = ExitStack()
    sb = lambda n, s, d=F32: es.enter_context(nc.sbuf_tensor(n, s, d))
    ps = lambda n, s, d=F32: es.enter_context(nc.psum_tensor(n, s, d))

    xt = [sb(f"xt{c}", [128, TW]) for c in range(8)]
    ft = xt
    bt = [sb(f"bt{c}", [128, TW], BF16) for c in range(8)]
    tA = sb("tA", [128, TW]); tB = sb("tB", [128, TW]); rest = sb("rest", [128, TW])
    rowA = sb("rowA", [1, TW]); rowB = sb("rowB", [1, TW]); rowC = sb("rowC", [1, TW])
    colA = sb("colA", [128, 1]); colB = sb("colB", [128, 1]); colC = sb("colC", [128, 1])
    onesr = sb("onesr", [1, 128]); onesc = sb("onesc", [128, 1])
    identb = sb("identb", [128, 128], BF16)
    identf = sb("identf", [128, 128])
    mbandf = sb("mbandf", [128, 1920])
    ltrif = sb("ltrif", [128, 128])
    rowself = sb("rowself", [128, 32])
    hsell = sb("hsell", [10, 1]); hselr = sb("hselr", [10, 1])
    wbig = sb("wbig", [128, 16384], BF16)
    wbig2 = sb("wbig2", [128, 8192], BF16)
    wst = sb("wst", [128, 2048])
    wtile = sb("wtile", [128, 1024], BF16)
    ktile = sb("ktile", [128, 1024], BF16)
    qt2 = sb("qt2", [128, 1024], BF16)
    vext = sb("vext", [128, 8 * 1040], BF16)
    stw = sb("stw", [128, TW], BF16)
    estf = sb("estf", [128, TW])
    zrecr = sb("zrecr", [1, TW])
    zreps = sb("zreps", [64, TW])
    aow = sb("aow", [128, 4096], BF16)
    gkg = sb("gkg", [128, 2048], BF16)
    gvx = sb("gvx", [128, 2 * 1040], BF16)
    lngs = sb("lngs", [128, 2048], BF16)
    sfull = sb("sfull", [128, 128])
    cdb = sb("cdb", [128, 128]); cdu = sb("cdu", [128, 128])
    cde = cdu
    scorev = sb("scorev", [128, 128])
    selv = sb("selv", [128, 128]); tiev = sb("tiev", [128, 128]); cumv = sb("cumv", [128, 128])
    sc1 = sb("sc1", [1, 1]); sc2 = sb("sc2", [1, 1]); sc3 = sb("sc3", [1, 1])
    sc4 = sb("sc4", [1, 1]); sc5 = sb("sc5", [1, 1])
    sc6 = sb("sc6", [1, 1]); sc7 = sb("sc7", [1, 1])
    logk = sb("logk", [1, 1])

    halol = sb("halol", [1, 1024]); halor = sb("halor", [1, 1024])
    halold = sb("halold", [128, 8]); halord = sb("halord", [128, 8])
    hgat = sb("hgat", [10, 1024])
    kmrow = sb("kmrow", [128, 8])
    kmr = sb("kmr", [128, 1024])
    kmrowr = sb("kmrowr", [1, 1024])
    cosr = sb("cosr", [128, 512]); sinr = sb("sinr", [128, 512])
    iorow = sb("iorow", [1, 512])
    qtokf = sb("qtokf", [128, 1024])
    ktokf = sb("ktokf", [128, 1024])
    vtokb = sb("vtokb", [128, 1024], BF16)
    krpb = sb("krpb", [128, 1024], BF16)
    zcol = sb("zcol", [128, 1]); zdot = sb("zdot", [128, 64]); zall = sb("zall", [128, 16])
    kvf = sb("kvf", [64, 1024])
    kvb = sb("kvb", [128, 512], BF16)

    qi = sb("qi", [128, 512], I32)
    edget = sb("edget", [2, 1024])
    convw = sb("convw", [128, TW + 2])
    convw_s = sb("convw_s", [128, 24]); convb_s = sb("convb_s", [128, 8])
    gA = sb("gA", [128, 8]); gB = sb("gB", [128, 8])
    g16 = sb("g16", [128, 16]); g32 = sb("g32", [128, 32])

    pp = [ps(f"pp{i}", [128, TW]) for i in range(5)]
    pRow = ps("pRow", [2, TW])
    pT = ps("pT", [128, 128], BF16)
    pK = ps("pK", [128, TW])

    es.enter_context(nc.allow_non_contiguous_dma(reason="tiny halo-edge DMAs"))
    gd = es.enter_context(nc.semaphore("gd"))
    gc = es.enter_context(nc.semaphore("gc"))
    block = es.enter_context(nc.Block())

    with es:
        V = 'vector'; S = 'scalar'; T = 'tensor'; G = 'gpsimd'

        def c_(eng, fn):
            P.comp(eng, fn)

        pA, pB, pC, pD, pE = pp
        pR1 = pRow[0:1, :]
        pR2 = pE[64:65, :]

        P.dma(identb[:], ident_bf)
        P.dma(identf[:], ident_f)
        P.dma(mbandf[:], maskband)
        P.dma(ltrif[:], ltri)
        P.dma(rowself[:], rowsel)
        P.dma(hsell[:], hsel_l)
        P.dma(hselr[:], hsel_r)
        P.dma(iorow[:], iota512)
        c_(V, lambda e: e.memset(onesr[:], 1.0))
        c_(V, lambda e: e.memset(onesc[:], 1.0))

        def rowsum_scalar(src128x1, dst1x1):
            c_(T, lambda e: e.matmul(pR1[:, 0:1], onesc[:, 0:1], src128x1[:, 0:1], start=True, stop=True))
            c_(S, lambda e: e.activation(dst1x1[:], pR1[:, 0:1], ACT.Copy))

        def rep_col(src1x1, dstcol):
            c_(T, lambda e: e.matmul(pA[:, 0:1], onesr[0:1, :], src1x1[:], start=True, stop=True))
            c_(S, lambda e: e.activation(dstcol[:], pA[:, 0:1], ACT.Copy))

        def rep_row(srcrow, dst, n):
            c_(T, lambda e: e.matmul(pB[:, 0:n], onesr[0:1, :], srcrow[0:1, 0:n], start=True, stop=True))
            c_(S, lambda e: e.activation(dst[:, 0:n], pB[:, 0:n], ACT.Copy))

        def ln_core(src_tiles, width=TW):
            w = width
            for c in range(8):
                c_(T, lambda e, c=c: e.matmul(pR1[:, 0:w], onesc[:, 0:1], src_tiles[c][:, 0:w],
                                              start=(c == 0), stop=(c == 7)))
            c_(S, lambda e: e.activation(rowA[:, 0:w], pR1[:, 0:w], ACT.Copy, scale=1.0 / 1024.0))
            for c in range(8):
                c_(S, lambda e, c=c: e.activation(tA[:, 0:w], src_tiles[c][:, 0:w], ACT.Square))
                c_(T, lambda e, c=c: e.matmul(pR2[:, 0:w], onesc[:, 0:1], tA[:, 0:w],
                                              start=(c == 0), stop=(c == 7)))
            c_(S, lambda e: e.activation(rowB[:, 0:w], pR2[:, 0:w], ACT.Copy, scale=1.0 / 1024.0))
            c_(V, lambda e: e.tensor_mul(rowC[:, 0:w], rowA[:, 0:w], rowA[:, 0:w]))
            c_(V, lambda e: e.tensor_sub(rowB[:, 0:w], rowB[:, 0:w], rowC[:, 0:w]))
            c_(V, lambda e: e.tensor_scalar(rowB[:, 0:w], rowB[:, 0:w], float(EPS_LN), None, ALU.add))
            c_(S, lambda e: e.activation(rowB[:, 0:w], rowB[:, 0:w], ACT.Sqrt))
            c_(V, lambda e: e.reciprocal(rowB[:, 0:w], rowB[:, 0:w]))
            c_(T, lambda e: e.matmul(pA[:, 0:w], onesr[0:1, :], rowA[:, 0:w], start=True, stop=True))
            c_(T, lambda e: e.matmul(pB[:, 0:w], onesr[0:1, :], rowB[:, 0:w], start=True, stop=True))
            c_(S, lambda e: e.activation(tA[:, 0:w], pA[:, 0:w], ACT.Copy))
            c_(S, lambda e: e.activation(tB[:, 0:w], pB[:, 0:w], ACT.Copy))

        def load_weight_bf(wap, dout, dest, kchunks=8, col0=0, ncols=None, kc0=0):
            nco = dout if ncols is None else ncols
            for c in range(kchunks):
                P.dma(wst[:, 0:nco], wap[(kc0 + c) * 128:(kc0 + c + 1) * 128, col0:col0 + nco])
                c_(V, lambda e, c=c, nco=nco, dest=dest: e.tensor_copy(
                    dest[:, c * nco:(c + 1) * nco], wst[:, 0:nco]))

        def halo_ex(idx, src_chunks):
            for c in range(8):
                P.dma(edget[0:1, c * 128:(c + 1) * 128],
                      src_chunks[c, :, 0:1].rearrange("p o -> (p o)").rearrange("(a b) -> a b", a=1), slow=True)
                P.dma(edget[1:2, c * 128:(c + 1) * 128],
                      src_chunks[c, :, NT - 1:NT].rearrange("p o -> (p o)").rearrange("(a b) -> a b", a=1), slow=True)
            P.dma(h_in[idx], edget[:])
            P.coll("AllGather", ALU.bypass, [h_in[idx]], [h_out[idx]])
            c_(V, lambda e: e.memset(hgat[:], 0.0))
            P.dma(hgat[1:9, :], h_out[idx])
            for half in range(2):
                sl = slice(half * 512, half * 512 + 512)
                c_(T, lambda e, sl=sl: e.matmul(pR1[:, 0:512], hsell[:, 0:1], hgat[:, sl], start=True, stop=True))
                c_(S, lambda e, sl=sl: e.activation(halol[0:1, sl], pR1[:, 0:512], ACT.Copy))
                c_(T, lambda e, sl=sl: e.matmul(pR2[:, 0:512], hselr[:, 0:1], hgat[:, sl], start=True, stop=True))
                c_(S, lambda e, sl=sl: e.activation(halor[0:1, sl], pR2[:, 0:512], ACT.Copy))
            for c in range(8):
                c_(T, lambda e, c=c: e.transpose(pA[:, 0:1], halol[0:1, c * 128:(c + 1) * 128], identf[0:1, 0:1]))
                c_(S, lambda e, c=c: e.activation(halold[:, c:c + 1], pA[0:128, 0:1], ACT.Copy))
                c_(T, lambda e, c=c: e.transpose(pB[:, 0:1], halor[0:1, c * 128:(c + 1) * 128], identf[0:1, 0:1]))
                c_(S, lambda e, c=c: e.activation(halord[:, c:c + 1], pB[0:128, 0:1], ACT.Copy))

        def conv_tile(src_chunks, c, j, wdm, bdm, dst_tile):
            P.dma(convw[:, 1:TW + 1], src_chunks[c, :, j * TW:(j + 1) * TW])
            if j > 0:
                P.dma(convw[:, 0:1], src_chunks[c, :, j * TW - 1:j * TW])
            else:
                c_(V, lambda e, c=c: e.tensor_copy(convw[:, 0:1], halold[:, c:c + 1]))
            if j < NTILE - 1:
                P.dma(convw[:, TW + 1:TW + 2], src_chunks[c, :, (j + 1) * TW:(j + 1) * TW + 1])
            else:
                c_(V, lambda e, c=c: e.tensor_copy(convw[:, TW + 1:TW + 2], halord[:, c:c + 1]))
            c_(V, lambda e, c=c: e.tensor_scalar(dst_tile[:], convw[:, 0:TW], wdm[:, 3 * c:3 * c + 1], None, ALU.mult))
            c_(V, lambda e, c=c: e.scalar_tensor_tensor(dst_tile[:], convw[:, 1:TW + 1], wdm[:, 3 * c + 1:3 * c + 2],
                                                        dst_tile[:], ALU.mult, ALU.add))
            c_(V, lambda e, c=c: e.scalar_tensor_tensor(dst_tile[:], convw[:, 2:TW + 2], wdm[:, 3 * c + 2:3 * c + 3],
                                                        dst_tile[:], ALU.mult, ALU.add))
            c_(V, lambda e, c=c: e.tensor_scalar(dst_tile[:], dst_tile[:], bdm[:, c:c + 1], None, ALU.add))

        for j in range(NTILE):
            for c in range(8):
                P.dma(xt[c][:], x0[c, :, j * TW:(j + 1) * TW], batch=True)
            ln_core(xt)
            for c in range(8):
                c_(V, lambda e, c=c: e.tensor_sub(ft[c][:], xt[c][:], tA[:]))
                c_(V, lambda e, c=c: e.tensor_mul(ft[c][:], ft[c][:], tB[:]))
                c_(V, lambda e, c=c: e.tensor_copy(bt[c][:], ft[c][:]))
                P.dma(LN0[c, :, j * TW:(j + 1) * TW], bt[c][:])
        for c in range(8):
            P.dma(xt[c][:, 0:256], xg[c, :, :])
        ln_core(xt, width=256)
        for c in range(8):
            c_(V, lambda e, c=c: e.tensor_sub(xt[c][:, 0:256], xt[c][:, 0:256], tA[:, 0:256]))
            c_(V, lambda e, c=c: e.tensor_tensor(lngs[:, c * 256:(c + 1) * 256],
                                                 xt[c][:, 0:256], tB[:, 0:256], ALU.mult))

        load_weight_bf(wq, 1024, wbig)
        for c in range(8):
            P.dma(wst[:, 0:1024], wk[c * 128:(c + 1) * 128, :])
            c_(V, lambda e, c=c: e.tensor_copy(wbig[:, 8192 + c * 1024: 8192 + (c + 1) * 1024], wst[:, 0:1024]))
        load_weight_bf(wv, 1024, wbig2)

        for j in range(NTILE):
            for c in range(8):
                P.dma(bt[c][:], LN0[c, :, j * TW:(j + 1) * TW], batch=True)
            for (wsb, off, dst) in ((wbig, 0, Qb), (wbig, 8192, Kb), (wbig2, 0, Vb)):
                for m in range(8):
                    for c in range(8):
                        c_(T, lambda e, c=c, m=m, wsb=wsb, off=off: e.matmul(
                            pA[:], wsb[:, off + c * 1024 + m * 128: off + c * 1024 + m * 128 + 128],
                            bt[c][:], start=(c == 0), stop=(c == 7)))
                    c_(S, lambda e: e.activation(stw[:], pA[:], ACT.Copy))
                    P.dma(dst[m, :, j * TW:(j + 1) * TW], stw[:])
        for (wsb, off, isgv) in ((wbig, 8192, False), (wbig2, 0, True)):
            for m in range(8):
                for c in range(8):
                    c_(T, lambda e, c=c, m=m, wsb=wsb, off=off: e.matmul(
                        pA[:, 0:256], wsb[:, off + c * 1024 + m * 128: off + c * 1024 + m * 128 + 128],
                        lngs[:, c * 256:(c + 1) * 256], start=(c == 0), stop=(c == 7)))
                if not isgv:
                    c_(S, lambda e, m=m: e.activation(gkg[:, m * 256:(m + 1) * 256], pA[:, 0:256], ACT.Copy))
                else:
                    c_(S, lambda e, m=m: e.activation(stw[:, 0:256], pA[:, 0:256], ACT.Copy))
                    for g2 in range(2):
                        c_(T, lambda e, g2=g2: e.transpose(pT[:], stw[:, g2 * 128:(g2 + 1) * 128], identb[:]))
                        for hh in range(2):
                            h = 2 * m + hh
                            c_(S, lambda e, g2=g2, hh=hh, h=h: e.activation(
                                gvx[:, g2 * 1040 + h * 65: g2 * 1040 + h * 65 + 64],
                                pT[:, hh * 64: hh * 64 + 64], ACT.Copy))
        for g2 in range(2):
            c_(V, lambda e, g2=g2: e.memset(gvx[:, g2 * 1040 + 64: g2 * 1040 + 1040:65], 1.0))

        for j in range(NTILE):
            for c in range(8):
                P.dma(bt[c][:], Qb[c, :, j * TW:(j + 1) * TW], batch=True)
            for h in range(16):
                ch, hh = h // 2, h % 2
                rows = slice(hh * 64, hh * 64 + 64)
                for g2 in range(2):
                    c_(T, lambda e, ch=ch, rows=rows, g2=g2: e.matmul(
                        pA[:], gkg[rows, ch * 256 + g2 * 128: ch * 256 + g2 * 128 + 128],
                        bt[ch][rows, :], start=True, stop=True))
                    c_(S, lambda e: e.activation(stw[:], pA[:], ACT.Exp, scale=0.125))
                    c_(T, lambda e, h=h, g2=g2: e.matmul(
                        pC[0:65, :], gvx[:, g2 * 1040 + h * 65: g2 * 1040 + h * 65 + 65],
                        stw[:], start=(g2 == 0), stop=(g2 == 1)))
                c_(V, lambda e: e.reciprocal(zrecr[:], pC[64:65, :]))
                c_(T, lambda e: e.matmul(pD[0:64, :], onesr[0:1, 0:64], zrecr[:], start=True, stop=True))
                c_(S, lambda e: e.activation(zreps[:], pD[0:64, :], ACT.Copy))
                c_(V, lambda e, ch=ch, rows=rows: e.tensor_mul(ft[ch][rows, :], pC[0:64, :], zreps[:]))
            for c in range(8):
                P.dma(AOG[c, :, j * TW:(j + 1) * TW], ft[c][:])

        for w in range(4):
            w0 = w * 1024
            for c in range(8):
                P.dma(wtile[:], Vb[c, :, w0:w0 + 1024])
                for kt in range(8):
                    c_(T, lambda e, kt=kt: e.transpose(
                        pT[:], wtile[:, kt * 128:(kt + 1) * 128], identb[:]))
                    for hh in range(2):
                        h = 2 * c + hh
                        c_(S, lambda e, kt=kt, h=h, hh=hh: e.activation(
                            vext[:, kt * 1040 + h * 65: kt * 1040 + h * 65 + 64],
                            pT[:, hh * 64: hh * 64 + 64], ACT.Copy))
            for kt in range(8):
                c_(V, lambda e, kt=kt: e.memset(vext[:, kt * 1040 + 64: kt * 1040 + 1040:65], 1.0))
            for qh in range(2):
                for h in range(16):
                    ch, hh = h // 2, h % 2
                    rows = slice(hh * 64, hh * 64 + 64)
                    if hh == 0:
                        P.dma(ktile[:], Kb[ch, :, w0:w0 + 1024])
                        P.dma(qt2[:], Qb[ch, :, w0:w0 + 1024])
                    for kt in range(8):
                        u0 = qh * 512 - kt * 128 + 896
                        c_(T, lambda e, rows=rows, kt=kt, qh=qh: e.matmul(
                            pA[:], ktile[rows, kt * 128: kt * 128 + 128],
                            qt2[rows, qh * 512: qh * 512 + 512],
                            start=True, stop=True))
                        c_(S, lambda e: e.activation(estf[:], pA[:], ACT.Exp, scale=0.125))
                        c_(V, lambda e, u0=u0: e.tensor_tensor(
                            stw[:], estf[:], mbandf[:, u0:u0 + 512], ALU.mult))
                        c_(T, lambda e, h=h, kt=kt: e.matmul(
                            pC[0:65, :], vext[:, kt * 1040 + h * 65: kt * 1040 + h * 65 + 65],
                            stw[:], start=(kt == 0), stop=(kt == 7)))
                    c_(V, lambda e: e.reciprocal(zrecr[:], pC[64:65, :]))
                    c_(T, lambda e: e.matmul(pD[0:64, :], onesr[0:1, 0:64], zrecr[:], start=True, stop=True))
                    c_(S, lambda e: e.activation(zreps[:], pD[0:64, :], ACT.Copy))
                    c_(V, lambda e, ch=ch, rows=rows: e.tensor_mul(
                        aow[rows, ch * 512:(ch + 1) * 512], pC[0:64, :], zreps[:]))
                for c in range(8):
                    P.dma(tA[:], AOG[c, :, w0 + qh * 512: w0 + qh * 512 + 512])
                    c_(V, lambda e, c=c: e.tensor_copy(tB[:], aow[:, c * 512:(c + 1) * 512]))
                    c_(V, lambda e: e.tensor_add(tA[:], tA[:], tB[:]))
                    c_(V, lambda e: e.tensor_copy(bt[0][:], tA[:]))
                    P.dma(AOB[c, :, w0 + qh * 512: w0 + qh * 512 + 512], bt[0][:])

        load_weight_bf(wo, 1024, wbig)
        for j in range(NTILE):
            for c in range(8):
                P.dma(bt[c][:], AOB[c, :, j * TW:(j + 1) * TW], batch=True)
            for m in range(8):
                for c in range(8):
                    c_(T, lambda e, c=c, m=m: e.matmul(
                        pA[:], wbig[:, c * 1024 + m * 128: c * 1024 + m * 128 + 128],
                        bt[c][:], start=(c == 0), stop=(c == 7)))
                c_(S, lambda e, m=m: e.activation(ft[m][:], pA[:], ACT.Copy))
            ln_core(ft)
            for c in range(8):
                c_(V, lambda e, c=c: e.tensor_sub(ft[c][:], ft[c][:], tA[:]))
                c_(V, lambda e, c=c: e.tensor_mul(ft[c][:], ft[c][:], tB[:]))
                P.dma(rest[:], x0[c, :, j * TW:(j + 1) * TW])
                c_(V, lambda e, c=c: e.tensor_add(ft[c][:], ft[c][:], rest[:]))
                P.dma(X1[c, :, j * TW:(j + 1) * TW], ft[c][:])

        P.dma(g16[:, 0:8], rt_dm)
        P.dma(gA[:], gl_dm)
        P.dma(gB[:], gh_dm)
        for j in range(NTILE):
            for c in range(8):
                P.dma(xt[c][:], X1[c, :, j * TW:(j + 1) * TW], batch=True)
            ln_core(xt)
            for c in range(8):
                c_(V, lambda e, c=c: e.tensor_sub(ft[c][:], xt[c][:], tA[:]))
                c_(V, lambda e, c=c: e.tensor_mul(ft[c][:], ft[c][:], tB[:]))
            for c in range(8):
                c_(T, lambda e, c=c: e.matmul(pR1[:], g16[:, c:c + 1], ft[c][:],
                                              start=(c == 0), stop=(c == 7)))
            c_(S, lambda e: e.activation(rowA[:], pR1[:], ACT.Copy))
            P.dma(s_in[0:1, j * TW:(j + 1) * TW], rowA[:])
            for c in range(8):
                c_(S, lambda e, c=c: e.activation(tA[:], ft[c][:], ACT.Square))
                c_(T, lambda e, c=c: e.matmul(pR2[:], onesc[:, 0:1], tA[:],
                                              start=(c == 0), stop=(c == 7)))
            c_(S, lambda e: e.activation(rowB[:], pR2[:], ACT.Sqrt))
            c_(V, lambda e: e.tensor_scalar(rowB[:], rowB[:], 1e-12, None, ALU.max))
            c_(V, lambda e: e.reciprocal(rowB[:], rowB[:]))
            c_(V, lambda e: e.tensor_scalar(rowB[:], rowB[:], 32.0, None, ALU.mult))
            rep_row(rowB, tB, TW)
            for c in range(8):
                c_(V, lambda e, c=c: e.tensor_mul(tA[:], ft[c][:], tB[:]))
                c_(V, lambda e, c=c: e.tensor_scalar(bt[0][:], tA[:], gA[:, c:c + 1], None, ALU.mult))
                P.dma(LIN[c, :, j * TW:(j + 1) * TW], bt[0][:])
                c_(V, lambda e, c=c: e.tensor_scalar(bt[1][:], tA[:], gB[:, c:c + 1], None, ALU.mult))
                P.dma(HIN[c, :, j * TW:(j + 1) * TW], bt[1][:])

        P.coll("AllGather", ALU.bypass, [s_in], [s_out])
        P.dma(sfull[:], s_out.rearrange("o (p f) -> (o p) f", p=128))
        c_(V, lambda e: e.memset(sc1[:], 1152.0))
        c_(S, lambda e: e.activation(logk[:], sc1[:], ACT.Ln))
        c_(V, lambda e: e.tensor_scalar(cdb[:], sfull[:], -1.0, None, ALU.mult))
        for it in range(50):
            c_(V, lambda e: e.tensor_add(cdu[:], sfull[:], cdb[:]))
            c_(V, lambda e: e.reduce_max(colC[:], cdu[:], AX))
            c_(T, lambda e: e.transpose(pA[0:1, 0:128], colC[:], identf[:]))
            c_(S, lambda e: e.activation(rowC[0:1, 0:128], pA[0:1, 0:128], ACT.Copy))
            c_(V, lambda e: e.reduce_max(sc4[:], rowC[0:1, 0:128], AX))
            rep_col(sc4, colC)
            c_(V, lambda e: e.tensor_scalar(cdu[:], cdu[:], colC[:, 0:1], None, ALU.subtract))
            c_(S, lambda e: e.activation(cde[:], cdu[:], ACT.Exp))
            c_(V, lambda e: e.reduce_sum(colA[:], cde[:], AX))
            rowsum_scalar(colA, sc2)
            c_(S, lambda e: e.activation(sc2[:], sc2[:], ACT.Ln))
            c_(V, lambda e: e.tensor_add(sc2[:], sc2[:], sc4[:]))
            c_(V, lambda e: e.tensor_sub(sc3[:], logk[:], sc2[:]))
            rep_col(sc3, colB)
            c_(V, lambda e: e.tensor_scalar(cdu[:], sfull[:], colB[:, 0:1], None, ALU.add))
            c_(V, lambda e: e.tensor_scalar(cdb[:], cdu[:], 0.0, -1.0, ALU.max, ALU.mult))
        c_(V, lambda e: e.tensor_scalar(cdu[:], sfull[:], colB[:, 0:1], None, ALU.add))
        c_(V, lambda e: e.tensor_add(cdu[:], cdu[:], cdb[:]))
        c_(S, lambda e: e.activation(scorev[:], cdu[:], ACT.Exp))
        c_(V, lambda e: e.memset(sc1[:], 0.0))
        c_(V, lambda e: e.memset(sc2[:], 1.125))
        for it in range(60):
            c_(V, lambda e: e.tensor_add(sc3[:], sc1[:], sc2[:]))
            c_(V, lambda e: e.tensor_scalar(sc3[:], sc3[:], 0.5, None, ALU.mult))
            rep_col(sc3, colC)
            c_(V, lambda e: e.tensor_scalar(cde[:], scorev[:], colC[:, 0:1], None, ALU.is_ge))
            c_(V, lambda e: e.reduce_sum(colA[:], cde[:], AX))
            rowsum_scalar(colA, sc4)
            c_(V, lambda e: e.tensor_scalar(sc4[:], sc4[:], 1024.0, None, ALU.is_ge))
            c_(V, lambda e: e.tensor_scalar(sc5[:], sc4[:], -1.0, 1.0, ALU.mult, ALU.add))
            c_(V, lambda e: e.tensor_mul(sc6[:], sc4[:], sc3[:]))
            c_(V, lambda e: e.tensor_mul(sc7[:], sc5[:], sc1[:]))
            c_(V, lambda e: e.tensor_add(sc1[:], sc6[:], sc7[:]))
            c_(V, lambda e: e.tensor_mul(sc6[:], sc4[:], sc2[:]))
            c_(V, lambda e: e.tensor_mul(sc7[:], sc5[:], sc3[:]))
            c_(V, lambda e: e.tensor_add(sc2[:], sc6[:], sc7[:]))
        rep_col(sc1, colC)
        c_(V, lambda e: e.tensor_scalar(selv[:], scorev[:], colC[:, 0:1], None, ALU.is_gt))
        c_(V, lambda e: e.reduce_sum(colA[:], selv[:], AX))
        rowsum_scalar(colA, sc4)
        c_(V, lambda e: e.tensor_scalar(sc4[:], sc4[:], -1.0, 1024.0, ALU.mult, ALU.add))
        c_(V, lambda e: e.tensor_scalar(tiev[:], scorev[:], colC[:, 0:1], None, ALU.is_equal))
        c_(V, lambda e: e.tensor_tensor_scan(cumv[:], tiev[:], tiev[:], 0.0, ALU.add, ALU.bypass))
        c_(V, lambda e: e.tensor_sub(cumv[:], cumv[:], tiev[:]))
        c_(V, lambda e: e.reduce_sum(colA[:], tiev[:], AX))
        c_(T, lambda e: e.matmul(pA[:, 0:1], ltrif[:], colA[:, 0:1], start=True, stop=True))
        c_(S, lambda e: e.activation(colB[:], pA[:, 0:1], ACT.Copy))
        c_(V, lambda e: e.tensor_scalar(cumv[:], cumv[:], colB[:, 0:1], None, ALU.add))
        rep_col(sc4, colC)
        c_(V, lambda e: e.tensor_scalar(cumv[:], cumv[:], colC[:, 0:1], None, ALU.is_lt))
        c_(V, lambda e: e.tensor_mul(tiev[:], tiev[:], cumv[:]))
        c_(V, lambda e: e.tensor_add(selv[:], selv[:], tiev[:]))
        c_(T, lambda e: e.matmul(pA[0:32, 0:128], rowself[:], selv[:], start=True, stop=True))
        c_(S, lambda e: e.activation(ft[0][0:32, 0:128], pA[0:32, 0:128], ACT.Copy))
        P.dma(selbuf, ft[0][0:32, 0:128])
        P.dma(selrow, selbuf.rearrange("p f -> (p f)").rearrange("(a b) -> a b", a=1))

        load_weight_bf(w1lT, 512, wbig)
        P.dma(gA[:, 0:4], b1l_dm)
        for j in range(NTILE):
            for c in range(8):
                P.dma(bt[c][:], LIN[c, :, j * TW:(j + 1) * TW], batch=True)
            for m in range(4):
                for c in range(8):
                    c_(T, lambda e, c=c, m=m: e.matmul(
                        pA[:], wbig[:, c * 512 + m * 128: c * 512 + m * 128 + 128],
                        bt[c][:], start=(c == 0), stop=(c == 7)))
                c_(S, lambda e, m=m: e.activation(stw[:], pA[:], ACT.Gelu, bias=gA[:, m:m + 1]))
                P.dma(HL[m, :, j * TW:(j + 1) * TW], stw[:])

        P.dma(g32[:], b1h_dm)
        for half in range(2):
            load_weight_bf(w1hT, 4096, wbig, col0=half * 2048, ncols=2048)
            for j in range(NTILE):
                for c in range(8):
                    P.dma(bt[c][:], HIN[c, :, j * TW:(j + 1) * TW], batch=True)
                for ml in range(16):
                    m = half * 16 + ml
                    for c in range(8):
                        c_(T, lambda e, c=c, ml=ml: e.matmul(
                            pA[:], wbig[:, c * 2048 + ml * 128: c * 2048 + ml * 128 + 128],
                            bt[c][:], start=(c == 0), stop=(c == 7)))
                    c_(S, lambda e, m=m: e.activation(stw[:], pA[:], ACT.Gelu, bias=g32[:, m:m + 1]))
                    P.dma(HH[m, :, j * TW:(j + 1) * TW], stw[:])

        def halfk32(src_chunks, w_ap, finish_cb):
            for halfk in range(2):
                load_weight_bf(w_ap, 1024, wbig, kchunks=16, kc0=halfk * 16)
                for j in range(NTILE):
                    for mh in range(2):
                        for kg in range(2):
                            for c in range(8):
                                P.dma(bt[c][:], src_chunks[halfk * 16 + kg * 8 + c, :, j * TW:(j + 1) * TW], batch=True)
                            for m4 in range(4):
                                m = mh * 4 + m4
                                for c in range(8):
                                    kl = kg * 8 + c
                                    c_(T, lambda e, kl=kl, c=c, m=m, m4=m4, kg=kg: e.matmul(
                                        pp[m4][:],
                                        wbig[:, kl * 1024 + m * 128: kl * 1024 + m * 128 + 128],
                                        bt[c][:], start=(kg == 0 and c == 0),
                                        stop=(kg == 1 and c == 7)))
                        for m4 in range(4):
                            m = mh * 4 + m4
                            if halfk == 0:
                                c_(S, lambda e, m4=m4: e.activation(tA[:], pp[m4][:], ACT.Copy))
                                P.dma(PART[m, :, j * TW:(j + 1) * TW], tA[:])
                            else:
                                finish_cb(j, m, m4)

        load_weight_bf(w2lT, 1024, wbig2, kchunks=4)
        P.dma(gA[:], b2l_dm)
        P.dma(gB[:], b2h_dm)
        for j in range(NTILE):
            for c in range(4):
                P.dma(bt[c][:], HL[c, :, j * TW:(j + 1) * TW], batch=True)
            for m in range(8):
                for c in range(4):
                    c_(T, lambda e, c=c, m=m: e.matmul(
                        pE[:], wbig2[:, c * 1024 + m * 128: c * 1024 + m * 128 + 128],
                        bt[c][:], start=(c == 0), stop=(c == 3)))
                c_(S, lambda e, m=m: e.activation(tA[:], pE[:], ACT.Identity, bias=gA[:, m:m + 1]))
                P.dma(LOUT[m, :, j * TW:(j + 1) * TW], tA[:])

        _h7_state = {}

        def heavy_cb(j, m, m4):
            if _h7_state.get('j') != j:
                _h7_state['j'] = j
                P.dma(rowC[:], selrow[0:1, j * TW:(j + 1) * TW])
                c_(T, lambda e: e.matmul(pE[:], onesr[0:1, :], rowC[:], start=True, stop=True))
                c_(S, lambda e: e.activation(estf[:], pE[:], ACT.Copy))
            c_(S, lambda e, m=m, m4=m4: e.activation(tA[:], pp[m4][:], ACT.Identity, bias=gB[:, m:m + 1]))
            P.dma(rest[:], PART[m, :, j * TW:(j + 1) * TW])
            c_(V, lambda e: e.tensor_add(tA[:], tA[:], rest[:]))
            c_(V, lambda e: e.tensor_mul(tA[:], tA[:], estf[:]))
            P.dma(rest[:], LOUT[m, :, j * TW:(j + 1) * TW])
            c_(V, lambda e, m=m: e.tensor_add(ft[m][:], tA[:], rest[:]))
            if m == 7:
                ln_core(ft)
                for c in range(8):
                    c_(V, lambda e, c=c: e.tensor_sub(ft[c][:], ft[c][:], tA[:]))
                    c_(V, lambda e, c=c: e.tensor_mul(ft[c][:], ft[c][:], tB[:]))
                    P.dma(rest[:], X1[c, :, j * TW:(j + 1) * TW])
                    c_(V, lambda e, c=c: e.tensor_add(ft[c][:], ft[c][:], rest[:]))
                    P.dma(X2[c, :, j * TW:(j + 1) * TW], ft[c][:])
        halfk32(HH, w2hT, heavy_cb)

        for j in range(NTILE):
            for c in range(8):
                P.dma(xt[c][:], X2[c, :, j * TW:(j + 1) * TW], batch=True)
            ln_core(xt)
            for c in range(8):
                c_(V, lambda e, c=c: e.tensor_sub(ft[c][:], xt[c][:], tA[:]))
                c_(V, lambda e, c=c: e.tensor_mul(ft[c][:], ft[c][:], tB[:]))
                P.dma(XN[c, :, j * TW:(j + 1) * TW], ft[c][:])
        halo_ex(0, XN)
        P.dma(convw_s[:], cpe1w); P.dma(convb_s[:], cpe1b)
        P.dma(gA[:], n1g); P.dma(gB[:], n1b)
        for j in range(NTILE):
            for c in range(8):
                conv_tile(XN, c, j, convw_s, convb_s, tB)
                P.dma(xt[c][:], XN[c, :, j * TW:(j + 1) * TW])
                c_(V, lambda e, c=c: e.tensor_add(xt[c][:], xt[c][:], tB[:]))
                P.dma(XC[c, :, j * TW:(j + 1) * TW], xt[c][:])
            ln_core(xt)
            for c in range(8):
                c_(V, lambda e, c=c: e.tensor_sub(ft[c][:], xt[c][:], tA[:]))
                c_(V, lambda e, c=c: e.tensor_mul(ft[c][:], ft[c][:], tB[:]))
                c_(V, lambda e, c=c: e.tensor_scalar(bt[0][:], ft[c][:], gA[:, c:c + 1],
                                                     gB[:, c:c + 1], ALU.mult, ALU.add))
                P.dma(XN1[c, :, j * TW:(j + 1) * TW], bt[0][:])

        load_weight_bf(actw, 1024, wbig)
        load_weight_bf(inw, 1024, wbig2)
        P.dma(gA[:], actb); P.dma(gB[:], inb)
        for j in range(NTILE):
            for c in range(8):
                P.dma(bt[c][:], XN1[c, :, j * TW:(j + 1) * TW], batch=True)
            for m in range(8):
                for c in range(8):
                    c_(T, lambda e, c=c, m=m: e.matmul(
                        pA[:], wbig[:, c * 1024 + m * 128: c * 1024 + m * 128 + 128],
                        bt[c][:], start=(c == 0), stop=(c == 7)))
                c_(S, lambda e, m=m: e.activation(stw[:], pA[:], ACT.Silu, bias=gA[:, m:m + 1]))
                P.dma(ACTR[m, :, j * TW:(j + 1) * TW], stw[:])
                for c in range(8):
                    c_(T, lambda e, c=c, m=m: e.matmul(
                        pB[:], wbig2[:, c * 1024 + m * 128: c * 1024 + m * 128 + 128],
                        bt[c][:], start=(c == 0), stop=(c == 7)))
                c_(S, lambda e, m=m: e.activation(ft[0][:], pB[:], ACT.Identity, bias=gB[:, m:m + 1]))
                P.dma(XI[m, :, j * TW:(j + 1) * TW], ft[0][:])

        halo_ex(1, XI)
        P.dma(convw_s[:], dwcw); P.dma(convb_s[:], dwcb)
        for j in range(NTILE):
            for c in range(8):
                conv_tile(XI, c, j, convw_s, convb_s, tB)
                c_(S, lambda e: e.activation(tB[:], tB[:], ACT.Silu))
                P.dma(XIS[c, :, j * TW:(j + 1) * TW], tB[:])
                c_(V, lambda e: e.tensor_copy(bt[0][:], tB[:]))
                P.dma(XISb[c, :, j * TW:(j + 1) * TW], bt[0][:])

        load_weight_bf(qkw, 2048, wbig)
        P.dma(g16[:], qkb)
        for j in range(NTILE):
            for c in range(8):
                P.dma(bt[c][:], XISb[c, :, j * TW:(j + 1) * TW], batch=True)
            for m in range(16):
                for c in range(8):
                    c_(T, lambda e, c=c, m=m: e.matmul(
                        pA[:], wbig[:, c * 2048 + m * 128: c * 2048 + m * 128 + 128],
                        bt[c][:], start=(c == 0), stop=(c == 7)))
                c_(S, lambda e, m=m: e.activation(ft[0][:], pA[:], ACT.Identity, bias=g16[:, m:m + 1]))
                c_(S, lambda e: e.activation(tA[:], ft[0][:], ACT.Exp))
                c_(V, lambda e: e.tensor_scalar(tB[:], ft[0][:], 0.0, None, ALU.is_gt))
                c_(V, lambda e: e.tensor_scalar(ft[0][:], ft[0][:], 1.0, None, ALU.add))
                c_(V, lambda e: e.tensor_mul(rest[:], tB[:], ft[0][:]))
                c_(V, lambda e: e.tensor_scalar(tB[:], tB[:], -1.0, 1.0, ALU.mult, ALU.add))
                c_(V, lambda e: e.tensor_mul(tA[:], tB[:], tA[:]))
                c_(V, lambda e: e.tensor_add(rest[:], rest[:], tA[:]))
                P.dma(QKE[m, :, j * TW:(j + 1) * TW], rest[:])

        for c in range(8):
            for hf in range(2):
                P.dma(wst[:], QKE[8 + c, :, hf * 2048:(hf + 1) * 2048])
                if hf == 0:
                    c_(V, lambda e, c=c: e.reduce_sum(kmrow[:, c:c + 1], wst[:], AX))
                else:
                    c_(V, lambda e: e.reduce_sum(colA[:], wst[:], AX))
                    c_(V, lambda e, c=c: e.tensor_add(kmrow[:, c:c + 1], kmrow[:, c:c + 1], colA[:]))
        P.dma(km_in, kmrow[:])
        P.coll("AllReduce", ALU.add, [km_in], [km_out])
        P.dma(kmrow[:], km_out)
        c_(V, lambda e: e.tensor_scalar(kmrow[:], kmrow[:], 1.0 / 16384.0, None, ALU.mult))
        for c in range(8):
            c_(T, lambda e, c=c: e.transpose(pA[0:1, 0:128], kmrow[:, c:c + 1], identf[:]))
            c_(S, lambda e, c=c: e.activation(kmrowr[0:1, c * 128:(c + 1) * 128], pA[0:1, 0:128], ACT.Copy))
        rep_row(kmrowr, kmr, 512)
        c_(T, lambda e: e.matmul(pB[:], onesr[0:1, :], kmrowr[0:1, 512:1024], start=True, stop=True))
        c_(S, lambda e: e.activation(kmr[:, 512:1024], pB[:], ACT.Copy))

        c_(V, lambda e: e.memset(sc5[:], 10000.0))
        c_(S, lambda e: e.activation(sc5[:], sc5[:], ACT.Ln))
        c_(V, lambda e: e.tensor_scalar(sc5[:], sc5[:], -1.0 / 512.0, None, ALU.mult))
        c_(V, lambda e: e.tensor_scalar(iorow[:], iorow[:], sc5[0:1, 0:1], None, ALU.mult))
        c_(S, lambda e: e.activation(iorow[:], iorow[:], ACT.Exp))
        c_(S, lambda e: e.activation(rowA[:, 0:512], iorow[:], ACT.Sin))
        c_(V, lambda e: e.tensor_scalar(rowC[:], iorow[:], -1.0, float(math.pi / 2), ALU.mult, ALU.add))
        c_(S, lambda e: e.activation(rowB[:, 0:512], rowC[:], ACT.Sin))
        rep_row(rowA, sinr, 512)
        rep_row(rowB, cosr, 512)

        def rope_inplace(dst):
            ev = dst[:, 0:1024:2]; od = dst[:, 1:1024:2]
            c_(V, lambda e, ev=ev: e.tensor_mul(tA[:, 0:512], ev, cosr[:]))
            c_(V, lambda e, ev=ev: e.tensor_mul(estf[:, 0:512], ev, sinr[:]))
            c_(V, lambda e, od=od: e.tensor_mul(tB[:, 0:512], od, sinr[:]))
            c_(V, lambda e, ev=ev: e.tensor_sub(ev, tA[:, 0:512], tB[:, 0:512]))
            c_(V, lambda e, od=od: e.tensor_mul(tB[:, 0:512], od, cosr[:]))
            c_(V, lambda e, od=od: e.tensor_add(od, estf[:, 0:512], tB[:, 0:512]))

        for t in range(32):
            t0 = t * 128
            for c in range(8):
                P.dma(xt[0][:, 0:128], QKE[c, :, t0:t0 + 128])
                c_(T, lambda e: e.transpose(pA[:, 0:128], xt[0][:, 0:128], identf[:]))
                c_(S, lambda e, c=c: e.activation(qtokf[:, c * 128:(c + 1) * 128], pA[:, 0:128], ACT.Copy))
                P.dma(xt[0][:, 0:128], QKE[8 + c, :, t0:t0 + 128])
                c_(T, lambda e: e.transpose(pB[:, 0:128], xt[0][:, 0:128], identf[:]))
                c_(S, lambda e, c=c: e.activation(ktokf[:, c * 128:(c + 1) * 128], pB[:, 0:128], ACT.Copy))
                P.dma(bt[0][:, 0:128], XISb[c, :, t0:t0 + 128])
                c_(T, lambda e: e.transpose(pT[:], bt[0][:, 0:128], identb[:]))
                c_(S, lambda e, c=c: e.activation(vtokb[:, c * 128:(c + 1) * 128], pT[:], ACT.Copy))
            for h in range(16):
                c_(V, lambda e, h=h: e.tensor_mul(zdot[:], qtokf[:, h * 64:(h + 1) * 64], kmr[:, h * 64:(h + 1) * 64]))
                c_(V, lambda e: e.reduce_sum(zcol[:], zdot[:], AX))
                c_(V, lambda e: e.tensor_scalar(zcol[:], zcol[:], 1e-6, 16384.0, ALU.add, ALU.mult))
                c_(V, lambda e, h=h: e.reciprocal(zall[:, h:h + 1], zcol[:]))
            rope_inplace(qtokf)
            rope_inplace(ktokf)
            c_(V, lambda e: e.tensor_copy(krpb[:], ktokf[:]))
            for h in range(16):
                c_(V, lambda e, h=h: e.tensor_scalar(qtokf[:, h * 64:(h + 1) * 64],
                                                     qtokf[:, h * 64:(h + 1) * 64],
                                                     zall[:, h:h + 1], None, ALU.mult))
            for c in range(8):
                c_(V, lambda e, c=c: e.tensor_copy(bt[1][:, 0:128], qtokf[:, c * 128:(c + 1) * 128]))
                c_(T, lambda e: e.transpose(pT[:], bt[1][:, 0:128], identb[:]))
                c_(S, lambda e, c=c, t0=t0: e.activation(stw[:, 0:128], pT[:], ACT.Copy))
                P.dma(QRS[c, :, t0:t0 + 128], stw[:, 0:128])
            for h in range(16):
                pk_rows = slice(0, 64) if h < 8 else slice(64, 128)
                col = (h % 8) * 64
                c_(T, lambda e, h=h, pk_rows=pk_rows, col=col, t=t: e.matmul(
                    pK[pk_rows, col:col + 64], krpb[:, h * 64:(h + 1) * 64],
                    vtokb[:, h * 64:(h + 1) * 64], start=(t == 0), stop=(t == 31),
                    skip_group_check=True))

        c_(S, lambda e: e.activation(kvf[:, 0:512], pK[0:64, :], ACT.Copy))
        c_(S, lambda e: e.activation(kvf[:, 512:1024], pK[64:128, :], ACT.Copy))
        P.dma(kv_in, kvf[:])
        P.coll("AllReduce", ALU.add, [kv_in], [kv_out])
        P.dma(kvf[:], kv_out)
        for h in range(16):
            ch, hh = h // 2, h % 2
            c_(V, lambda e, h=h, ch=ch, hh=hh: e.tensor_copy(
                kvb[hh * 64:hh * 64 + 64, ch * 64:(ch + 1) * 64], kvf[0:64, h * 64:(h + 1) * 64]))

        halo_ex(2, XIS)
        P.dma(convw_s[:], lepw); P.dma(convb_s[:], lepb)
        for j in range(NTILE):
            for c in range(8):
                P.dma(bt[0][:], QRS[c, :, j * TW:(j + 1) * TW])
                for hh in range(2):
                    c_(T, lambda e, c=c, hh=hh: e.matmul(
                        pA[hh * 64:hh * 64 + 64, :],
                        kvb[hh * 64:hh * 64 + 64, c * 64:(c + 1) * 64],
                        bt[0][hh * 64:hh * 64 + 64, :], start=True, stop=True))
                conv_tile(XIS, c, j, convw_s, convb_s, tB)
                c_(V, lambda e: e.tensor_add(ft[0][:], pA[:], tB[:]))
                P.dma(bt[1][:], ACTR[c, :, j * TW:(j + 1) * TW])
                c_(V, lambda e: e.tensor_copy(rest[:], bt[1][:]))
                c_(V, lambda e: e.tensor_mul(ft[0][:], ft[0][:], rest[:]))
                c_(V, lambda e: e.tensor_copy(bt[2][:], ft[0][:]))
                P.dma(AMUL[c, :, j * TW:(j + 1) * TW], bt[2][:])

        load_weight_bf(outw, 1024, wbig)
        P.dma(gA[:], outb)
        for j in range(NTILE):
            for c in range(8):
                P.dma(bt[c][:], AMUL[c, :, j * TW:(j + 1) * TW], batch=True)
            for m in range(8):
                for c in range(8):
                    c_(T, lambda e, c=c, m=m: e.matmul(
                        pA[:], wbig[:, c * 1024 + m * 128: c * 1024 + m * 128 + 128],
                        bt[c][:], start=(c == 0), stop=(c == 7)))
                c_(S, lambda e, m=m: e.activation(ft[0][:], pA[:], ACT.Identity, bias=gA[:, m:m + 1]))
                P.dma(rest[:], XC[m, :, j * TW:(j + 1) * TW])
                c_(V, lambda e: e.tensor_add(ft[0][:], ft[0][:], rest[:]))
                P.dma(XM[m, :, j * TW:(j + 1) * TW], ft[0][:])

        halo_ex(3, XM)
        P.dma(convw_s[:], cpe2w); P.dma(convb_s[:], cpe2b)
        P.dma(gA[:], n2g); P.dma(gB[:], n2b)
        for j in range(NTILE):
            for c in range(8):
                conv_tile(XM, c, j, convw_s, convb_s, tB)
                P.dma(xt[c][:], XM[c, :, j * TW:(j + 1) * TW])
                c_(V, lambda e, c=c: e.tensor_add(xt[c][:], xt[c][:], tB[:]))
                P.dma(XM2[c, :, j * TW:(j + 1) * TW], xt[c][:])
            ln_core(xt)
            for c in range(8):
                c_(V, lambda e, c=c: e.tensor_sub(ft[c][:], xt[c][:], tA[:]))
                c_(V, lambda e, c=c: e.tensor_mul(ft[c][:], ft[c][:], tB[:]))
                c_(V, lambda e, c=c: e.tensor_scalar(bt[0][:], ft[c][:], gA[:, c:c + 1],
                                                     gB[:, c:c + 1], ALU.mult, ALU.add))
                P.dma(FCIN[c, :, j * TW:(j + 1) * TW], bt[0][:])

        P.dma(g32[:], fc1b)
        for half in range(2):
            load_weight_bf(fc1w, 4096, wbig, col0=half * 2048, ncols=2048)
            for j in range(NTILE):
                for c in range(8):
                    P.dma(bt[c][:], FCIN[c, :, j * TW:(j + 1) * TW], batch=True)
                for ml in range(16):
                    m = half * 16 + ml
                    for c in range(8):
                        c_(T, lambda e, c=c, ml=ml: e.matmul(
                            pA[:], wbig[:, c * 2048 + ml * 128: c * 2048 + ml * 128 + 128],
                            bt[c][:], start=(c == 0), stop=(c == 7)))
                    c_(S, lambda e, m=m: e.activation(stw[:], pA[:], ACT.Gelu, bias=g32[:, m:m + 1]))
                    P.dma(H1[m, :, j * TW:(j + 1) * TW], stw[:])

        P.dma(gA[:], fc2b)

        def fc2_cb(j, m, m4):
            c_(S, lambda e, m=m, m4=m4: e.activation(tA[:], pp[m4][:], ACT.Identity, bias=gA[:, m:m + 1]))
            P.dma(rest[:], PART[m, :, j * TW:(j + 1) * TW])
            c_(V, lambda e: e.tensor_add(tA[:], tA[:], rest[:]))
            P.dma(rest[:], XM2[m, :, j * TW:(j + 1) * TW])
            c_(V, lambda e, m=m: e.tensor_add(ft[m][:], tA[:], rest[:]))
            if m == 7:
                ln_core(ft)
                for c in range(8):
                    c_(V, lambda e, c=c: e.tensor_sub(ft[c][:], ft[c][:], tA[:]))
                    c_(V, lambda e, c=c: e.tensor_mul(ft[c][:], ft[c][:], tB[:]))
                    P.dma(rest[:], X2[c, :, j * TW:(j + 1) * TW])
                    c_(V, lambda e, c=c: e.tensor_add(ft[c][:], ft[c][:], rest[:]))
                    P.dma(X3[c, :, j * TW:(j + 1) * TW], ft[c][:])
        halfk32(H1, fc2w, fc2_cb)

        for j in range(NTILE):
            for c in range(8):
                P.dma(xt[c][:], X3[c, :, j * TW:(j + 1) * TW], batch=True)
            ln_core(xt)
            for c in range(8):
                c_(V, lambda e, c=c: e.tensor_sub(ft[c][:], xt[c][:], tA[:]))
                c_(V, lambda e, c=c: e.tensor_mul(ft[c][:], ft[c][:], tB[:]))
                P.dma(XN4[c, :, j * TW:(j + 1) * TW], ft[c][:])
        for c in range(8):
            P.dma(wst[:, 0:1024], wT[c * 128:(c + 1) * 128, :])
            c_(V, lambda e, c=c: e.reduce_sum(kmrow[:, c:c + 1], wst[:, 0:1024], AX,
                                              apply_absolute_value=True))
        c_(V, lambda e: e.reduce_sum(colA[:], kmrow[:], AX))
        rowsum_scalar(colA, sc1)
        c_(V, lambda e: e.tensor_scalar(sc1[:], sc1[:], 1.0 / (1024.0 * 1024.0), 1e-8, ALU.mult, ALU.add))
        c_(V, lambda e: e.reciprocal(sc3[:], sc1[:]))
        rep_col(sc3, colB)
        for c in range(8):
            P.dma(wst[:, 0:1024], wT[c * 128:(c + 1) * 128, :])
            c_(V, lambda e: e.tensor_scalar(wst[:, 0:1024], wst[:, 0:1024], colB[:, 0:1], None, ALU.mult))
            for hf in range(2):
                P.dma(qi[:], wst[:, hf * TW:(hf + 1) * TW], eng='gpsimd')
                P.dma(wst[:, hf * TW:(hf + 1) * TW], qi[:], eng='gpsimd')
                c_(V, lambda e, hf=hf: e.tensor_scalar(tA[:, 0:TW], wst[:, hf * TW:(hf + 1) * TW],
                                                       -1.0, 1.0, ALU.max, ALU.min))
                c_(V, lambda e, c=c, hf=hf: e.tensor_copy(wbig2[:, c * 1024 + hf * TW: c * 1024 + (hf + 1) * TW],
                                                          tA[:, 0:TW]))
        for t in range(32):
            t0 = t * 128
            for c in range(8):
                P.dma(bt[0][:, 0:128], XN4[c, :, t0:t0 + 128], eng='gpsimd')
                c_(T, lambda e: e.transpose(pT[:], bt[0][:, 0:128], identb[:]))
                c_(S, lambda e, c=c: e.activation(qtokf[:, c * 128:(c + 1) * 128], pT[:], ACT.Copy))
            c_(V, lambda e: e.tensor_reduce(zcol[:], qtokf[:], AX, ALU.max, apply_absolute_value=True))
            c_(V, lambda e: e.tensor_scalar(zcol[:], zcol[:], 1e-8, None, ALU.add))
            c_(T, lambda e: e.transpose(pA[0:1, 0:128], zcol[:], identf[:]))
            c_(S, lambda e: e.activation(rowA[0:1, 0:128], pA[0:1, 0:128], ACT.Copy))
            P.dma(arow_h[0:1, t0:t0 + 128], rowA[0:1, 0:128])
        c_(V, lambda e: e.tensor_scalar(sc2[:], sc1[:], 1.0 / 127.0, None, ALU.mult))
        for j in range(NTILE):
            P.dma(rowC[:], arow_h[0:1, j * TW:(j + 1) * TW])
            c_(T, lambda e: e.matmul(pE[:], onesr[0:1, :], rowC[:], start=True, stop=True))
            c_(S, lambda e: e.activation(rest[:], pE[:], ACT.Copy))
            c_(V, lambda e: e.reciprocal(rest[:], rest[:]))
            for c in range(8):
                P.dma(xt[c][:], XN4[c, :, j * TW:(j + 1) * TW], batch=True)
                c_(V, lambda e, c=c: e.scalar_tensor_tensor(xt[c][:], xt[c][:], 127.0, rest[:],
                                                            ALU.mult, ALU.mult))
                c_(V, lambda e, c=c: e.tensor_scalar(xt[c][:], xt[c][:], -128.0, 127.0, ALU.max, ALU.min))
                P.dma(qi[:], xt[c][:], eng='gpsimd')
                P.dma(xt[c][:], qi[:], eng='gpsimd')
                c_(V, lambda e, c=c: e.tensor_copy(bt[c][:], xt[c][:]))
            for m in range(8):
                for c in range(8):
                    c_(T, lambda e, c=c, m=m: e.matmul(
                        pA[:], wbig2[:, c * 1024 + m * 128: c * 1024 + m * 128 + 128],
                        bt[c][:], start=(c == 0), stop=(c == 7)))
                c_(S, lambda e, m=m: e.activation(ft[m][:], pA[:], ACT.Copy))
                c_(V, lambda e: e.tensor_scalar(rowB[:], rowC[:], sc2[0:1, 0:1], None, ALU.mult))
                c_(T, lambda e: e.matmul(pE[:], onesr[0:1, :], rowB[:], start=True, stop=True))
                c_(S, lambda e: e.activation(tA[:], pE[:], ACT.Copy))
                c_(V, lambda e, m=m: e.tensor_mul(ft[m][:], ft[m][:], tA[:]))
            ln_core(ft)
            for c in range(8):
                c_(V, lambda e, c=c: e.tensor_sub(ft[c][:], ft[c][:], tA[:]))
                c_(V, lambda e, c=c: e.tensor_mul(ft[c][:], ft[c][:], tB[:]))
                P.dma(rest[:], X3[c, :, j * TW:(j + 1) * TW])
                c_(V, lambda e, c=c: e.tensor_add(ft[c][:], ft[c][:], rest[:]))
                P.dma(out[c, :, j * TW:(j + 1) * TW], ft[c][:])

        P.emit(block, gd, gc)
    return P


_NC_CACHE = {}


def _build_nc():
    if 'nc' not in _NC_CACHE:
        nc = bass.Bass("TRN2", target_bir_lowering=False)
        build(nc)
        _NC_CACHE['nc'] = nc
    return _NC_CACHE['nc']


def kernel(**inputs):
    import ml_dtypes
    nc = _build_nc()

    def dm(v, nchunk):
        return np.ascontiguousarray(np.asarray(v, np.float32).reshape(nchunk, 128).T)

    def cw(w):
        w = np.asarray(w, np.float32).reshape(-1, 3)
        return np.ascontiguousarray(w.reshape(8, 128, 3).transpose(1, 0, 2).reshape(128, 24))

    x = np.asarray(inputs["L0_attn_x"], np.float32)
    shared = {
        "wq": np.ascontiguousarray(inputs["L0_attn_wq"], np.float32),
        "wk": np.ascontiguousarray(inputs["L0_attn_wk"], np.float32),
        "wv": np.ascontiguousarray(inputs["L0_attn_wv"], np.float32),
        "wo": np.ascontiguousarray(inputs["L0_attn_wo"], np.float32),
        "rt_dm": dm(inputs["L0_ffn_routing_token"], 8),
        "gl_dm": dm(inputs["L0_ffn_gamma_l"], 8),
        "gh_dm": dm(inputs["L0_ffn_gamma_h"], 8),
        "w1lT": np.ascontiguousarray(np.asarray(inputs["L0_ffn_w1_l"], np.float32).T),
        "b1l_dm": dm(inputs["L0_ffn_b1_l"], 4),
        "w2lT": np.ascontiguousarray(np.asarray(inputs["L0_ffn_w2_l"], np.float32).T),
        "b2l_dm": dm(inputs["L0_ffn_b2_l"], 8),
        "w1hT": np.ascontiguousarray(np.asarray(inputs["L0_ffn_w1_h"], np.float32).T),
        "b1h_dm": dm(inputs["L0_ffn_b1_h"], 32),
        "w2hT": np.ascontiguousarray(np.asarray(inputs["L0_ffn_w2_h"], np.float32).T),
        "b2h_dm": dm(inputs["L0_ffn_b2_h"], 8),
        "cpe1w": cw(inputs["L1_attn_cpe1_w"]), "cpe1b": dm(inputs["L1_attn_cpe1_b"], 8),
        "n1g": dm(inputs["L1_attn_norm1_g"], 8), "n1b": dm(inputs["L1_attn_norm1_b"], 8),
        "inw": np.ascontiguousarray(inputs["L1_attn_in_w"], np.float32),
        "inb": dm(inputs["L1_attn_in_b"], 8),
        "actw": np.ascontiguousarray(inputs["L1_attn_actp_w"], np.float32),
        "actb": dm(inputs["L1_attn_actp_b"], 8),
        "dwcw": cw(inputs["L1_attn_dwc_w"]), "dwcb": dm(inputs["L1_attn_dwc_b"], 8),
        "qkw": np.ascontiguousarray(inputs["L1_attn_qk_w"], np.float32),
        "qkb": dm(inputs["L1_attn_qk_b"], 16),
        "lepw": cw(inputs["L1_attn_lepe_w"]), "lepb": dm(inputs["L1_attn_lepe_b"], 8),
        "outw": np.ascontiguousarray(inputs["L1_attn_out_w"], np.float32),
        "outb": dm(inputs["L1_attn_out_b"], 8),
        "cpe2w": cw(inputs["L1_attn_cpe2_w"]), "cpe2b": dm(inputs["L1_attn_cpe2_b"], 8),
        "n2g": dm(inputs["L1_attn_norm2_g"], 8), "n2b": dm(inputs["L1_attn_norm2_b"], 8),
        "fc1w": np.ascontiguousarray(inputs["L1_attn_fc1_w"], np.float32),
        "fc1b": dm(inputs["L1_attn_fc1_b"], 32),
        "fc2w": np.ascontiguousarray(inputs["L1_attn_fc2_w"], np.float32),
        "fc2b": dm(inputs["L1_attn_fc2_b"], 8),
        "wT": np.ascontiguousarray(np.asarray(inputs["L1_ffn_weight"], np.float32).T),
        "ident_bf": np.eye(128, dtype=ml_dtypes.bfloat16),
        "ident_f": np.eye(128, dtype=np.float32),
        "ltri": np.triu(np.ones((128, 128), np.float32), 1),
        "iota512": np.arange(512, dtype=np.float32).reshape(1, 512),
    }
    u = np.arange(1920)[None, :]
    kk = np.arange(128)[:, None]
    shared["maskband"] = ((u - 896) >= kk).astype(np.float32)

    in_maps = []
    for core in range(8):
        b, r = core // 4, core % 4
        t0 = r * 4096
        xs = x[b, t0:t0 + 4096, :]
        xdm = np.ascontiguousarray(xs.T.reshape(8, 128, 4096))
        xgs = np.ascontiguousarray(x[b, 0:256, :].T.reshape(8, 128, 256))
        rowsel = np.zeros((128, 32), np.float32)
        rowsel[np.arange(r * 32, r * 32 + 32), np.arange(32)] = 1.0
        hsl = np.zeros((10, 1), np.float32)
        hsl[2 * r if r > 0 else 0, 0] = 1.0 if r > 0 else 0.0
        hsr = np.zeros((10, 1), np.float32)
        if r < 3:
            hsr[2 * r + 3, 0] = 1.0
        m = dict(shared)
        m.update({"x": xdm, "xg": xgs, "rowsel": rowsel, "hsel_l": hsl, "hsel_r": hsr})
        in_maps.append(m)

    res = run_bass_kernel_spmd(nc, in_maps, core_ids=list(range(8)))
    outp = np.empty((2, 16384, 1024), np.float32)
    for core in range(8):
        b, r = core // 4, core % 4
        o = np.asarray(res.results[core]["out"])
        outp[b, r * 4096:(r + 1) * 4096, :] = o.reshape(1024, 4096).T
    return outp
